# version 05 (final)
# baseline (speedup 1.0000x reference)
import numpy as np
import concourse.bass as bass
import concourse.mybir as mybir
from contextlib import ExitStack

F32 = mybir.dt.float32
BF16 = mybir.dt.bfloat16
FP16 = mybir.dt.float16
A = mybir.AluOpType
AF = mybir.ActivationFunctionType
AX = mybir.AxisListType

MAGIC = 12582912.0
NEG_BIG = -1e30
EPS = 1e-5
T, TG, HG = 1024, 512, 256
D, KC = 2048, 16
H, HD, NCH = 16, 128, 8
I, IC = 8192, 64
NCORE = 8
RG = [list(range(NCORE))]

SL_SLOPES = 0
SL_PD = 16
SL_S1 = 32
SL_SCR = 32


def _emit_ln(nc, c, SEM, L, hg, col0):
    sx, sl, gp, mm, vv, sc, st = (SEM[k] for k in ["sx", "sl", "gp", "mm", "vv", "sc", "st"])
    xf, XFB, TB0, TB1, RMX, RMN, MB, RB, CB = (L[k] for k in
        ["xf", "XFB", "TB0", "TB1", "RMX", "RMN", "MB", "RB", "CB"])
    XNB, XQ, ONEC, ONE1, ROW, XSR, IDF, LNC, NEGC = (L[k] for k in
        ["XNB", "XQ", "ONEC", "ONE1", "ROW", "XSR", "IDF", "LNC", "NEGC"])
    pA = L["pA"]
    s_half = hg % 2
    MNr = ROW[0:1, SL_S1 + 0 * HG:SL_S1 + 1 * HG]
    RSr = ROW[0:1, SL_S1 + 1 * HG:SL_S1 + 2 * HG]
    r1 = ROW[0:1, SL_S1 + 2 * HG:SL_S1 + 3 * HG]
    r2 = ROW[0:1, SL_S1 + 3 * HG:SL_S1 + 4 * HG]
    r3 = ROW[0:1, SL_S1 + 4 * HG:SL_S1 + 5 * HG]

    with nc.Block() as blk:
        @blk.sync
        def _(s_):
            s_.wait_ge(vv, c.get("vv"))
            s_.dma_start(out=XFB[:], in_=xf[:, col0:col0 + HG].rearrange(
                "(a p) c -> p a c", p=128)).then_inc(sx, 16)
            c.add("sx", 16)

        @blk.tensor
        def _(t_):
            t_.wait_ge(sx, c.get("sx"))
            t_.wait_ge(vv, c.get("vv"))
            for k in range(KC):
                t_.matmul(pA[6][0:1, 0:HG], ONEC[:], XFB[:, k, :],
                          start=(k == 0), stop=(k == KC - 1))
            base_sc = c.get("sc")
            for k in range(KC):
                t_.wait_ge(sc, base_sc + k + 1)
                ins = t_.matmul(pA[7][0:1, 0:HG], ONEC[:], (TB0 if k % 2 == 0 else TB1)[:],
                                start=(k == 0), stop=(k == KC - 1))
                ins.then_inc(mm, 1)
                c.add("mm", 1)

        @blk.scalar
        def _(s_):
            s_.wait_ge(sx, c.get("sx"))
            base_mm = c.get("mm") - KC
            for k in range(KC):
                if k >= 2:
                    s_.wait_ge(mm, base_mm + k - 1)
                s_.activation((TB0 if k % 2 == 0 else TB1)[:], XFB[:, k, :],
                              AF.Square).then_inc(sc, 1)
                c.add("sc", 1)

    with nc.Block() as blk:
        @blk.vector
        def _(v_):
            v_.wait_ge(mm, c.get("mm"))
            v_.tensor_scalar(MNr, pA[6][0:1, 0:HG], 1.0 / 2048.0, None, A.mult)
            v_.tensor_scalar(r1, pA[7][0:1, 0:HG], 1.0 / 2048.0, None, A.mult)
            v_.tensor_tensor(r2, MNr, MNr, A.mult)
            v_.tensor_tensor(r3, r1, r2, A.subtract)
            v_.tensor_scalar(r3, r3, EPS, None, A.add).then_inc(vv, 1)
            c.add("vv", 1)

        @blk.scalar
        def _(s_):
            s_.wait_ge(vv, c.get("vv"))
            s_.activation(r1, r3, AF.Ln).then_inc(sc, 1)
            c.add("sc", 1)

        @blk.vector
        def _(v_):
            v_.wait_ge(sc, c.get("sc"))
            v_.tensor_scalar(r1, r1, -0.5, None, A.mult).then_inc(vv, 1)
            c.add("vv", 1)

        @blk.scalar
        def _(s_):
            s_.wait_ge(vv, c.get("vv"))
            s_.activation(RSr, r1, AF.Exp).then_inc(sc, 1)
            c.add("sc", 1)

        @blk.tensor
        def _(t_):
            t_.wait_ge(sc, c.get("sc"))
            t_.matmul(pA[2][:, 0:HG], ONE1[:], MNr, start=True, stop=True)
            t_.matmul(pA[3][:, 0:HG], ONE1[:], RSr, start=True, stop=True).then_inc(mm, 1)
            c.add("mm", 1)

        @blk.vector
        def _(v_):
            v_.wait_ge(mm, c.get("mm"))
            v_.tensor_copy(MB[:], pA[2][:, 0:HG])
            v_.tensor_copy(RB[:], pA[3][:, 0:HG]).then_inc(vv, 1)
            c.add("vv", 1)

    with nc.Block() as blk:
        @blk.vector
        def _(v_):
            for k in range(KC):
                tb = TB0 if k % 2 == 0 else TB1
                v_.tensor_tensor(tb[:], XFB[:, k, :], MB[:], A.subtract)
                v_.tensor_tensor(XNB[:, k, s_half * HG:(s_half + 1) * HG], tb[:], RB[:],
                                 A.mult)
                if k == 0:
                    v_.tensor_copy(RMX[:], tb[:])
                    v_.tensor_copy(RMN[:], tb[:])
                else:
                    v_.tensor_tensor(RMX[:], RMX[:], tb[:], A.max)
                    v_.tensor_tensor(RMN[:], RMN[:], tb[:], A.min)
            v_.tensor_scalar(RMN[:], RMN[:], -1.0, None, A.mult)
            v_.tensor_tensor(RMX[:], RMX[:], RMN[:], A.max).then_inc(vv, 1)
            c.add("vv", 1)

        @blk.tensor
        def _(t_):
            t_.wait_ge(vv, c.get("vv"))
            t_.matmul(pA[5][:, 0:128], RMX[:, 0:128], IDF[:], start=True, stop=True)
            t_.matmul(pA[6][:, 0:128], RMX[:, 128:256], IDF[:], start=True,
                      stop=True).then_inc(mm, 1)
            c.add("mm", 1)

        @blk.vector
        def _(v_):
            v_.wait_ge(mm, c.get("mm"))
            v_.reduce_max(LNC[:, 0:1], pA[5][:, 0:128], AX.X)
            v_.reduce_max(LNC[:, 1:2], pA[6][:, 0:128], AX.X).then_inc(vv, 1)
            c.add("vv", 1)

        @blk.tensor
        def _(t_):
            t_.wait_ge(vv, c.get("vv"))
            t_.matmul(pA[7][0:1, 0:128], LNC[:, 0:1], IDF[:], start=True, stop=True)
            t_.matmul(pA[7][0:1, 128:256], LNC[:, 1:2], IDF[:], start=True,
                      stop=True).then_inc(mm, 1)
            c.add("mm", 1)

        @blk.vector
        def _(v_):
            v_.wait_ge(mm, c.get("mm"))
            v_.tensor_tensor(r1, pA[7][0:1, 0:HG], RSr, A.mult)
            v_.tensor_scalar(XSR[0:1, col0:col0 + HG], r1, 1.0 / 127.0, EPS, A.mult,
                             A.add).then_inc(vv, 1)
            c.add("vv", 1)

        @blk.scalar
        def _(s_):
            s_.wait_ge(vv, c.get("vv"))
            s_.activation(r1, XSR[0:1, col0:col0 + HG], AF.Ln)
            s_.activation(r2, r1, AF.Exp, scale=NEGC[0:1, 0:1]).then_inc(sc, 1)
            c.add("sc", 1)

        @blk.vector
        def _(v_):
            v_.wait_ge(sc, c.get("sc"))
            v_.tensor_tensor(r3, RSr, r2, A.mult).then_inc(vv, 1)
            c.add("vv", 1)

        @blk.tensor
        def _(t_):
            t_.wait_ge(vv, c.get("vv"))
            t_.matmul(pA[4][:, 0:HG], ONE1[:], r3, start=True, stop=True).then_inc(mm, 1)
            c.add("mm", 1)

        @blk.vector
        def _(v_):
            v_.wait_ge(mm, c.get("mm"))
            v_.tensor_copy(CB[:], pA[4][:, 0:HG])
            for k in range(KC):
                tb = TB0 if k % 2 == 0 else TB1
                v_.tensor_tensor(tb[:], XFB[:, k, :], MB[:], A.subtract)
                v_.tensor_tensor(tb[:], tb[:], CB[:], A.mult)
                v_.tensor_scalar(XQ[:, k, col0:col0 + HG], tb[:], MAGIC, MAGIC,
                                 A.add, A.subtract)
            v_.engine_nop().then_inc(vv, 1)
            c.add("vv", 1)


def _emit_qkv(nc, c, SEM, L, g):
    sx, sl, gp, mm, vv, sc, cc = (SEM[k] for k in ["sx", "sl", "gp", "mm", "vv", "sc", "ccc"])
    slp = [SEM["sl0"], SEM["sl1"]]
    wqkvb, WS0, WS1, XNB, QF, KF, VT, BPP, BVB = (L[k] for k in
        ["wqkvb", "WS0", "WS1", "XNB", "QF", "KF", "VT", "BPP", "BVB"])
    pA = L["pA"]
    WS = [WS0, WS1]
    NS = 48
    base_sl0 = c.get("sl0")
    base_sl1 = c.get("sl1")
    base_mm = c.get("mm")
    base_vv = c.get("vv")

    with nc.Block() as blk:
        @blk.sync
        def _(s_):
            s_.wait_ge(cc, c.v["cc_cast"])
            for s in range(0, NS, 2):
                if s >= 2:
                    s_.wait_ge(mm, base_mm + s - 1)
                s_.dma_start(out=WS[0][:],
                             in_=wqkvb.ap()[:, s * 128:(s + 1) * 128].rearrange(
                                 "(a p) c -> p a c", p=128)).then_inc(slp[0], 16)
                c.add("sl0", 16)

        @blk.scalar
        def _(s_):
            s_.wait_ge(cc, c.v["cc_cast"])
            for s in range(1, NS, 2):
                if s >= 2:
                    s_.wait_ge(mm, base_mm + s - 1)
                s_.dma_start(out=WS[1][:],
                             in_=wqkvb.ap()[:, s * 128:(s + 1) * 128].rearrange(
                                 "(a p) c -> p a c", p=128)).then_inc(slp[1], 16)
                c.add("sl1", 16)

        @blk.tensor
        def _(t_):
            t_.wait_ge(vv, base_vv)
            for s in range(NS):
                bsl = base_sl0 if s % 2 == 0 else base_sl1
                t_.wait_ge(slp[s % 2], bsl + 16 * (s // 2 + 1))
                if s >= 2:
                    t_.wait_ge(vv, base_vv + s - 1)
                if s >= 33:
                    t_.wait_ge(vv, base_vv + s)
                ins = None
                if s < 32:
                    pb = pA[s % 2]
                    for k in range(KC):
                        ins = t_.matmul(pb[:, 0:512], WS[s % 2][:, k, :],
                                        XNB[:, k, :], start=(k == 0), stop=(k == KC - 1))
                else:
                    for tok in range(4):
                        pb = pA[4 + tok]
                        for k in range(KC):
                            ins = t_.matmul(pb[:, 0:128],
                                            XNB[:, k, tok * 128:(tok + 1) * 128],
                                            WS[s % 2][:, k, :],
                                            start=(k == 0), stop=(k == KC - 1))
                ins.then_inc(mm, 1)
                c.add("mm", 1)

        @blk.vector
        def _(v_):
            for s in range(NS):
                v_.wait_ge(mm, base_mm + s + 1)
                if s < 32:
                    pb = pA[s % 2]
                    if s < 16:
                        dst = QF[:, s, g * 512:(g + 1) * 512]
                    else:
                        dst = KF[:, s - 16, :]
                    v_.tensor_scalar(dst, pb[:, 0:512], BPP[:, s:s + 1], None, A.add)
                else:
                    of0 = (s - 32) * 128
                    for tok in range(4):
                        pb = pA[4 + tok]
                        v_.tensor_tensor(VT[:, tok, of0:of0 + 128], pb[:, 0:128],
                                         BVB[:, of0:of0 + 128], A.add)
                v_.engine_nop().then_inc(vv, 1)
                c.add("vv", 1)


def _emit_scan(nc, c, SEM, L, g):
    mm, vv, sc = SEM["mm"], SEM["vv"], SEM["sc"]
    QF, KF, VT, ATT, EB, MSK, SLB, KDA, LMA = (L[k] for k in
        ["QF", "KF", "VT", "ATT", "EB", "MSK", "SLB", "KDA", "LMA"])
    QDE, QD2, QDQ, STB, KTM, SBF, IDB = (L[k] for k in
        ["QDE", "QD2", "QDQ", "STB", "KTM", "SBF", "IDB"])
    pA = L["pA"]
    M0 = c.get("mm")
    V0 = c.get("vv")
    S0 = c.get("sc")

    with nc.Block() as blk:
        @blk.vector
        def _(v_):
            for h in range(H):
                if h >= 1:
                    v_.wait_ge(sc, S0 + h)
                v_.tensor_scalar(QD2[:], QDE[:], SLB[:, h:h + 1], None, A.mult
                                 ).then_inc(vv, 1)
                v_.wait_ge(sc, S0 + h + 1)
                if h >= 1:
                    v_.wait_ge(mm, M0 + 4 * h)
                for lc in range(4):
                    v_.tensor_tensor(QDQ[:, lc * 128:(lc + 1) * 128],
                                     QF[:, h, g * 512 + lc * 128:g * 512 + (lc + 1) * 128],
                                     QD2[:], A.mult)
                v_.engine_nop().then_inc(vv, 1)
                P = h % 2
                pAs, pAk, pAc, pAo = pA[4 * P], pA[1 + 4 * P], pA[2 + 4 * P], pA[3 + 4 * P]
                v_.wait_ge(mm, M0 + 4 * h + 1)
                for lc in range(4):
                    v_.tensor_tensor(STB[:, lc * 128:(lc + 1) * 128],
                                     pAs[:, lc * 128:(lc + 1) * 128], MSK[:, h, :], A.mult)
                v_.engine_nop().then_inc(vv, 1)
                v_.wait_ge(mm, M0 + 4 * h + 2)
                for lc in range(4):
                    v_.tensor_scalar(KTM[:, lc * 128:(lc + 1) * 128],
                                     pAk[:, lc * 128:(lc + 1) * 128], KDA[:, h:h + 1],
                                     None, A.mult)
                v_.engine_nop().then_inc(vv, 1)
                v_.wait_ge(mm, M0 + 4 * h + 3)
                for lc in range(4):
                    v_.tensor_copy(SBF[:, lc * 128:(lc + 1) * 128], EB[:, h, :])
                    v_.scalar_tensor_tensor(EB[:, h, :], EB[:, h, :], LMA[:, h:h + 1],
                                            pAc[:, lc * 128:(lc + 1) * 128],
                                            A.mult, A.add)
                v_.engine_nop().then_inc(vv, 1)
                v_.wait_ge(mm, M0 + 4 * h + 4)
                for lc in range(4):
                    osl = slice(g * 512 + lc * 128, g * 512 + (lc + 1) * 128)
                    v_.tensor_copy(ATT[:, h, osl], pAo[:, lc * 128:(lc + 1) * 128])
                v_.engine_nop().then_inc(vv, 1)
                c.add("vv", 6)

        @blk.scalar
        def _(s_):
            for h in range(H):
                s_.wait_ge(vv, V0 + 6 * h + 1)
                s_.activation(QD2[:], QD2[:], AF.Exp).then_inc(sc, 1)
                c.add("sc", 1)

        @blk.tensor
        def _(t_):
            t_.wait_ge(vv, V0)
            for h in range(H):
                P = h % 2
                pAs, pAk, pAc, pAo = pA[4 * P], pA[1 + 4 * P], pA[2 + 4 * P], pA[3 + 4 * P]
                if h >= 2:
                    t_.wait_ge(vv, V0 + 6 * (h - 2) + 3)
                for lc in range(4):
                    ksl = slice(lc * 128, (lc + 1) * 128)
                    qsl = slice(g * 512 + lc * 128, g * 512 + (lc + 1) * 128)
                    ins = t_.matmul(pAs[:, ksl], KF[:, h, ksl], QF[:, h, qsl],
                                    start=True, stop=True)
                ins.then_inc(mm, 1)
                if h >= 2:
                    t_.wait_ge(vv, V0 + 6 * (h - 2) + 4)
                for lc in range(4):
                    ksl = slice(lc * 128, (lc + 1) * 128)
                    ins = t_.matmul(pAk[:, ksl], KF[:, h, ksl], IDB[:],
                                    start=True, stop=True)
                ins.then_inc(mm, 1)
                t_.wait_ge(vv, V0 + 6 * h + 4)
                for lc in range(4):
                    ksl = slice(lc * 128, (lc + 1) * 128)
                    ins = t_.matmul(pAc[:, ksl], KTM[:, ksl],
                                    VT[:, lc, h * 128:(h + 1) * 128],
                                    start=True, stop=True)
                ins.then_inc(mm, 1)
                t_.wait_ge(vv, V0 + 6 * h + 5)
                for lc in range(4):
                    ksl = slice(lc * 128, (lc + 1) * 128)
                    t_.matmul(pAo[:, ksl], VT[:, lc, h * 128:(h + 1) * 128],
                              STB[:, ksl], start=True, stop=False)
                    ins = t_.matmul(pAo[:, ksl], SBF[:, ksl], QDQ[:, ksl],
                                    start=False, stop=True)
                ins.then_inc(mm, 1)
                c.add("mm", 4)


def _emit_prefix(nc, c, SEM, L):
    sx, sl, gp, mm, vv, sc, cc, st = (SEM[k] for k in
                                      ["sx", "sl", "gp", "mm", "vv", "sc", "cc", "st"])
    EB, EG, PDB, SLB, WPB, TBB, EHB, QGQ, SBF, QF, ATT = (L[k] for k in
        ["EB", "EG", "PDB", "SLB", "WPB", "TBB", "EHB", "QGQ", "SBF", "QF", "ATT"])
    e_in, e_out, pdist, qdgneg, ROW, ONE1, QGR = (L[k] for k in
        ["e_in", "e_out", "pdist", "qdgneg", "ROW", "ONE1", "QGR"])
    pA = L["pA"]
    NCORE = 8
    RG = [list(range(NCORE))]

    with nc.Block() as blk:
        @blk.gpsimd
        def _(g_):
            g_.wait_ge(vv, c.get("vv"))
            g_.dma_start(out=e_in.ap().rearrange("(h p) c -> p h c", p=128),
                         in_=EB[:]).then_inc(gp, 16)
            c.add("gp", 16)
            g_.wait_ge(gp, c.get("gp"))
            g_.collective_compute("AllGather", A.bypass, replica_groups=RG,
                                  ins=[e_in.ap().opt()], outs=[e_out.ap().opt()]
                                  ).then_inc(SEM["cce"], 1)
            c.add("cce", 1)

        @blk.sync
        def _(s_):
            s_.dma_start(out=ROW[0:1, SL_PD:SL_PD + 8], in_=pdist).then_inc(sx, 16)
            s_.dma_start(out=QGR[0:1, :], in_=qdgneg).then_inc(sx, 16)
            c.add("sx", 32)

        @blk.tensor
        def _(t_):
            t_.wait_ge(sx, c.get("sx"))
            t_.wait_ge(vv, c.get("vv"))
            t_.matmul(pA[6][:, 0:8], ONE1[:], ROW[0:1, SL_PD:SL_PD + 8],
                      start=True, stop=True)
            t_.matmul(pA[4][:, 0:512], ONE1[:], QGR[0:1, 0:512],
                      start=True, stop=True)
            t_.matmul(pA[5][:, 0:512], ONE1[:], QGR[0:1, 512:1024],
                      start=True, stop=True).then_inc(mm, 1)
            c.add("mm", 1)

        @blk.vector
        def _(v_):
            v_.wait_ge(mm, c.get("mm"))
            v_.tensor_copy(PDB[:], pA[6][:, 0:8])
            v_.tensor_copy(TBB[:, 0:512], pA[4][:, 0:512])
            v_.tensor_copy(TBB[:, 512:1024], pA[5][:, 0:512])
            v_.wait_ge(gp, c.get("gp"))
            v_.memset(EB[:], 0.0).then_inc(vv, 1)
            c.add("vv", 1)

    base_vv = c.get("vv")
    with nc.Block() as blk:
        @blk.gpsimd
        def _(g_):
            g_.wait_ge(SEM["cce"], c.get("cce"))
            g_.wait_ge(vv, base_vv)
            for p in range(NCORE):
                if p >= 1:
                    g_.wait_ge(vv, base_vv + 2 * p)
                g_.dma_start(out=EG[:],
                             in_=e_out.ap()[p * 2048:(p + 1) * 2048, :].rearrange(
                                 "(h q) c -> q h c", q=128)).then_inc(gp, 16)
            c.add("gp", 16 * NCORE)

        @blk.vector
        def _(v_):
            base_gp = c.get("gp") - 16 * NCORE
            base_sc = c.get("sc")
            for p in range(NCORE):
                v_.wait_ge(gp, base_gp + 16 * (p + 1))
                v_.tensor_scalar(WPB[:], SLB[:], PDB[:, p:p + 1], None, A.mult
                                 ).then_inc(vv, 1)
                c.add("vv", 1)
                v_.wait_ge(sc, base_sc + p + 1)
                for h in range(H):
                    v_.scalar_tensor_tensor(EB[:, h, :], EG[:, h, :], WPB[:, h:h + 1],
                                            EB[:, h, :], A.mult, A.add)
                v_.engine_nop().then_inc(vv, 1)
                c.add("vv", 1)

        @blk.scalar
        def _(s_):
            base_vv2 = c.get("vv") - 2 * NCORE
            for p in range(NCORE):
                s_.wait_ge(vv, base_vv2 + 2 * p + 1)
                s_.activation(WPB[:], WPB[:], AF.Exp).then_inc(sc, 1)
                c.add("sc", 1)

    for h in range(H):
        with nc.Block() as blk:
            @blk.vector
            def _(v_):
                v_.wait_ge(sc, c.get("sc"))
                v_.tensor_scalar(EHB[:], TBB[:], SLB[:, h:h + 1], None, A.mult
                                 ).then_inc(vv, 1)
                c.add("vv", 1)

            @blk.scalar
            def _(s_):
                s_.wait_ge(vv, c.get("vv"))
                s_.activation(EHB[:], EHB[:], AF.Exp).then_inc(sc, 1)
                c.add("sc", 1)

            @blk.vector
            def _(v_):
                v_.wait_ge(sc, c.get("sc"))
                if h > 0:
                    v_.wait_ge(mm, c.get("mm"))
                v_.tensor_tensor(QGQ[:], QF[:, h, :], EHB[:], A.mult)
                v_.tensor_copy(SBF[:, 0:128], EB[:, h, :]).then_inc(vv, 1)
                c.add("vv", 1)

            @blk.tensor
            def _(t_):
                t_.wait_ge(vv, c.get("vv"))
                if h > 0:
                    pass
                t_.matmul(pA[0][:, 0:512], SBF[:, 0:128], QGQ[:, 0:512],
                          start=True, stop=True)
                t_.matmul(pA[1][:, 0:512], SBF[:, 0:128], QGQ[:, 512:1024], start=True,
                          stop=True).then_inc(mm, 1)
                c.add("mm", 1)

            @blk.vector
            def _(v_):
                v_.wait_ge(mm, c.get("mm"))
                for tg in range(2):
                    sl_ = slice(tg * 512, (tg + 1) * 512)
                    v_.tensor_tensor(ATT[:, h, sl_], ATT[:, h, sl_],
                                     pA[tg][:, 0:512], A.add)
                v_.engine_nop().then_inc(vv, 1)
                c.add("vv", 1)


def _emit_wout(nc, c, SEM, L):
    sx, sl, gp, mm, vv, sc, cc, st = (SEM[k] for k in
                                      ["sx", "sl", "gp", "mm", "vv", "sc", "ccc", "st"])
    slp = [SEM["sl0"], SEM["sl1"]]
    sxp = [SEM["sx0"], SEM["sx1"]]
    stp = [SEM["st0"], SEM["st1"]]
    WO0, WO1, BOB, BOR, XT0, XT1, OB0, OB1, ATT, ONE1 = (L[k] for k in
        ["WO0", "WO1", "BOB", "BOR", "XT0", "XT1", "OB0", "OB1", "ATT", "ONE1"])
    woutb, xt, out, bout_row = L["woutb"], L["xt"], L["out"], L["bout_row"]
    pC = L["pC"]
    WO = [WO0, WO1]
    XT = [XT0, XT1]
    OB = [OB0, OB1]

    with nc.Block() as blk:
        @blk.sync
        def _(s_):
            s_.dma_start(out=BOR[0:1, :], in_=bout_row).then_inc(sx, 16)
            c.add("sx", 16)

        @blk.tensor
        def _(t_):
            t_.wait_ge(sx, c.get("sx"))
            t_.wait_ge(vv, c.get("vv"))
            ins = None
            for j in range(4):
                ins = t_.matmul(pC[4 + j][:, 0:512], ONE1[:],
                                BOR[0:1, j * 512:(j + 1) * 512], start=True, stop=True)
            ins.then_inc(mm, 1)
            c.add("mm", 1)

        @blk.vector
        def _(v_):
            v_.wait_ge(mm, c.get("mm"))
            for j in range(4):
                v_.tensor_copy(BOB[:, j * 512:(j + 1) * 512], pC[4 + j][:, 0:512])
            v_.engine_nop().then_inc(vv, 1)
            c.add("vv", 1)

    base_mm = c.get("mm")
    base_vv = c.get("vv")
    base_st0 = c.get("st0")
    base_st1 = c.get("st1")
    base_sx0 = c.get("sx0")
    base_sx1 = c.get("sx1")
    base_sl0 = c.get("sl0")
    base_sl1 = c.get("sl1")
    with nc.Block() as blk:
        @blk.sync
        def _(s_):
            s_.wait_ge(cc, c.v["cc_cast"])
            s_.dma_start(out=WO[0][:],
                         in_=woutb.ap()[:, 0:512].rearrange(
                             "(a p) c -> p a c", p=128)).then_inc(slp[0], 16)
            c.add("sl0", 16)
            for n in range(32):
                if n == 16:
                    s_.wait_ge(mm, base_mm + 8)
                    s_.dma_start(out=WO[0][:],
                                 in_=woutb.ap()[:, 1024:1536].rearrange(
                                     "(a p) c -> p a c", p=128)).then_inc(slp[0], 16)
                    c.add("sl0", 16)
                ofg, tok = n // 8, n % 8
                if n >= 2:
                    s_.wait_ge(vv, base_vv + n - 1)
                s_.dma_start(out=XT[n % 2][:],
                             in_=xt[tok * 128:(tok + 1) * 128,
                                    ofg * 512:(ofg + 1) * 512]).then_inc(sxp[n % 2], 16)
                c.add("sx%d" % (n % 2), 16)

        @blk.tensor
        def _(t_):
            t_.wait_ge(vv, base_vv)
            for ofg in range(4):
                bsl = base_sl0 if ofg % 2 == 0 else base_sl1
                t_.wait_ge(slp[ofg % 2], bsl + 16 * (ofg // 2 + 1))
                for tok in range(8):
                    n = 8 * ofg + tok
                    if n >= 2:
                        t_.wait_ge(vv, base_vv + n - 1)
                    for k in range(KC):
                        ins = t_.matmul(pC[n % 2][:, 0:512],
                                        ATT[:, k, tok * 128:(tok + 1) * 128],
                                        WO[ofg % 2][:, k, :],
                                        start=(k == 0), stop=(k == KC - 1))
                    ins.then_inc(mm, 1)
                    c.add("mm", 1)

        @blk.vector
        def _(v_):
            for ofg in range(4):
                osl = slice(ofg * 512, (ofg + 1) * 512)
                for tok in range(8):
                    n = 8 * ofg + tok
                    v_.wait_ge(mm, base_mm + n + 1)
                    bx = base_sx0 if n % 2 == 0 else base_sx1
                    v_.wait_ge(sxp[n % 2], bx + 16 * (n // 2 + 1))
                    if n >= 2:
                        bst = base_st0 if n % 2 == 0 else base_st1
                        v_.wait_ge(stp[n % 2], bst + 16 * (n // 2))
                    v_.tensor_tensor(OB[n % 2][:], pC[n % 2][:, 0:512], BOB[:, osl],
                                     A.add)
                    v_.tensor_tensor(OB[n % 2][:], OB[n % 2][:], XT[n % 2][:],
                                     A.add).then_inc(vv, 1)
                    c.add("vv", 1)

        @blk.scalar
        def _(s_):
            s_.wait_ge(cc, c.v["cc_cast"])
            s_.dma_start(out=WO[1][:],
                         in_=woutb.ap()[:, 512:1024].rearrange(
                             "(a p) c -> p a c", p=128)).then_inc(slp[1], 16)
            c.add("sl1", 16)
            for n in range(32):
                if n == 16:
                    s_.dma_start(out=WO[1][:],
                                 in_=woutb.ap()[:, 1536:2048].rearrange(
                                     "(a p) c -> p a c", p=128)).then_inc(slp[1], 16)
                    c.add("sl1", 16)
                ofg, tok = n // 8, n % 8
                s_.wait_ge(vv, base_vv + n + 1)
                s_.dma_start(out=out[tok * 128:(tok + 1) * 128,
                                     ofg * 512:(ofg + 1) * 512],
                             in_=OB[n % 2][:]).then_inc(stp[n % 2], 16)
                c.add("st%d" % (n % 2), 16)
    c.v["st_C0"] = c.get("st0")
    c.v["st_C1"] = c.get("st1")
    c.v["mm_C"] = c.get("mm")


def _emit_ffn(nc, c, SEM, L, tg):
    sx, sl, gp, mm, vv, sc, cc, st, acc = (SEM[k] for k in
        ["sx", "sl", "gp", "mm", "vv", "sc", "ccw", "st", "acc"])
    slp = [SEM["sl0"], SEM["sl1"]]
    sxp = [SEM["sx0"], SEM["sx1"]]
    stp = [SEM["st0"], SEM["st1"]]
    (HH, WG0, WG1, WU0, WU1, WD0, WD1, BG, BU, RSB, GB, UB, RXH, RNH, SHD) = (
        L[k] for k in ["HH", "WG0", "WG1", "WU0", "WU1", "WD0", "WD1", "BG", "BU",
                       "RSB", "GB", "UB", "RXH", "RNH", "SHD"])
    XQ, QF, ATT, XSR, WSR, ROW, ONE1, ONE11, IDF, DCL, WSC, NEGC, RC127, EPSC = (
        L[k] for k in
        ["XQ", "QF", "ATT", "XSR", "WSR", "ROW", "ONE1", "ONE11", "IDF", "DCL", "WSC",
         "NEGC", "RC127", "EPSC"])
    wqg, wqu, wqd, out = L["wqg"], L["wqu"], L["wqd"], L["out"]
    pD = L["pD"]
    WG = [WG0, WG1]
    WU = [WU0, WU1]
    WD = [WD0, WD1]
    tsl = slice(tg * 512, (tg + 1) * 512)
    r1 = ROW[0:1, SL_SCR:SL_SCR + 512]
    r2 = ROW[0:1, SL_SCR + 512:SL_SCR + 1024]

    def hq_ap(kk):
        buf = QF if kk < 32 else ATT
        a = kk if kk < 32 else kk - 32
        return buf[:, a // 2, (a % 2) * 512:(a % 2) * 512 + 512]

    with nc.Block() as blk:
        @blk.vector
        def _(v_):
            v_.tensor_scalar(r1, XSR[0:1, tsl], WSR[0:1, 0:1], None, A.mult)
            v_.tensor_scalar(r2, XSR[0:1, tsl], WSR[0:1, 1:2], None, A.mult
                             ).then_inc(vv, 1)
            c.add("vv", 1)

        @blk.tensor
        def _(t_):
            t_.wait_ge(vv, c.get("vv"))
            t_.matmul(pD[4][:, 0:512], ONE1[:], r1, start=True, stop=True)
            t_.matmul(pD[5][:, 0:512], ONE1[:], r2, start=True, stop=True).then_inc(mm, 1)
            c.add("mm", 1)

        @blk.vector
        def _(v_):
            v_.wait_ge(mm, c.get("mm"))
            v_.tensor_copy(BG[:], pD[4][:, 0:512])
            v_.tensor_copy(BU[:], pD[5][:, 0:512]).then_inc(vv, 1)
            c.add("vv", 1)

    base_sl0 = c.get("sl0")
    base_sl1 = c.get("sl1")
    base_mm = c.get("mm")
    base_vv = c.get("vv")
    base_sc = c.get("sc")
    acc_wait = c.v.get("acc_tg0", 0) if tg == 1 else 0
    base_sx0 = c.get("sx0")
    base_sx1 = c.get("sx1")
    with nc.Block() as blk:
        @blk.sync
        def _(s_):
            s_.wait_ge(cc, c.v["cc_weights"])
            if tg == 0:
                s_.wait_ge(stp[0], c.v["st_C0"])
                s_.wait_ge(stp[1], c.v["st_C1"])
            for ic in range(IC):
                if ic >= 2:
                    s_.wait_ge(mm, base_mm + ic - 1)
                s_.dma_start(out=WG[ic % 2][:],
                             in_=wqg.ap()[:, ic * 128:(ic + 1) * 128].rearrange(
                                 "(a p) c -> p a c", p=128)).then_inc(slp[ic % 2], 16)
                c.add("sl%d" % (ic % 2), 16)

        @blk.tensor
        def _(t_):
            t_.wait_ge(vv, base_vv)
            for ic in range(IC):
                bsl = base_sl0 if ic % 2 == 0 else base_sl1
                bsx = base_sx0 if ic % 2 == 0 else base_sx1
                t_.wait_ge(slp[ic % 2], bsl + 16 * (ic // 2 + 1))
                t_.wait_ge(sxp[ic % 2], bsx + 16 * (ic // 2 + 1))
                if ic >= 2:
                    t_.wait_ge(vv, base_vv + 2 * (ic - 1))
                par = ic % 2
                for k in range(KC):
                    t_.matmul(pD[par][:, 0:512], WG[par][:, k, :], XQ[:, k, tsl],
                              start=(k == 0), stop=(k == KC - 1))
                for k in range(KC):
                    ins = t_.matmul(pD[2 + par][:, 0:512], WU[par][:, k, :],
                                    XQ[:, k, tsl], start=(k == 0), stop=(k == KC - 1))
                ins.then_inc(mm, 1)
                c.add("mm", 1)

        @blk.vector
        def _(v_):
            if tg == 1:
                v_.wait_ge(acc, acc_wait)
            for ic in range(IC):
                par = ic % 2
                v_.wait_ge(mm, base_mm + ic + 1)
                if ic >= 1:
                    v_.wait_ge(sc, base_sc + ic)
                v_.tensor_tensor(GB[:], pD[par][:, 0:512], BG[:], A.mult
                                 ).then_inc(vv, 1)
                v_.tensor_tensor(UB[:], pD[2 + par][:, 0:512], BU[:], A.mult)
                v_.wait_ge(sc, base_sc + ic + 1)
                v_.tensor_tensor(HH[:, ic, :], GB[:], UB[:], A.mult)
                if ic == 0:
                    v_.tensor_copy(RXH[:], HH[:, ic, :])
                    v_.tensor_copy(RNH[:], HH[:, ic, :])
                else:
                    v_.tensor_tensor(RXH[:], RXH[:], HH[:, ic, :], A.max)
                    v_.tensor_tensor(RNH[:], RNH[:], HH[:, ic, :], A.min)
                v_.engine_nop().then_inc(vv, 1)
                c.add("vv", 2)

        @blk.scalar
        def _(s_):
            s_.wait_ge(cc, c.v["cc_weights"])
            if tg == 0:
                s_.wait_ge(stp[0], c.v["st_C0"])
                s_.wait_ge(stp[1], c.v["st_C1"])
            for icw in range(2):
                s_.dma_start(out=WU[icw][:],
                             in_=wqu.ap()[:, icw * 128:(icw + 1) * 128].rearrange(
                                 "(a p) c -> p a c", p=128)).then_inc(sxp[icw], 16)
                c.add("sx%d" % icw, 16)
            for ic in range(IC):
                s_.wait_ge(vv, base_vv + 2 * ic + 1)
                s_.activation(GB[:], GB[:], AF.Silu).then_inc(sc, 1)
                c.add("sc", 1)
                icw = ic + 2
                if icw < IC:
                    s_.wait_ge(mm, base_mm + ic + 1)
                    s_.dma_start(out=WU[icw % 2][:],
                                 in_=wqu.ap()[:, icw * 128:(icw + 1) * 128].rearrange(
                                     "(a p) c -> p a c", p=128)).then_inc(sxp[icw % 2], 16)
                    c.add("sx%d" % (icw % 2), 16)

    with nc.Block() as blk:
        @blk.vector
        def _(v_):
            v_.tensor_scalar(RNH[:], RNH[:], -1.0, None, A.mult)
            v_.tensor_tensor(RXH[:], RXH[:], RNH[:], A.max).then_inc(vv, 1)
            c.add("vv", 1)

        @blk.tensor
        def _(t_):
            t_.wait_ge(vv, c.get("vv"))
            ins = None
            for j in range(4):
                ins = t_.matmul(pD[4][:, j * 128:(j + 1) * 128],
                                RXH[:, j * 128:(j + 1) * 128], IDF[:],
                                start=True, stop=True)
            ins.then_inc(mm, 1)
            c.add("mm", 1)

        @blk.vector
        def _(v_):
            v_.wait_ge(mm, c.get("mm"))
            for j in range(3):
                v_.reduce_max(DCL[:, j:j + 1], pD[4][:, j * 128:(j + 1) * 128], AX.X)
            v_.reduce_max(DCL[:, 3:4], pD[4][:, 384:512], AX.X).then_inc(vv, 1)
            c.add("vv", 1)

        @blk.scalar
        def _(s_):
            s_.wait_ge(vv, c.get("vv"))
            s_.activation(DCL[:, 4:8], DCL[:, 0:4], AF.Abs,
                          bias=EPSC[:, 0:1], scale=RC127[:, 0:1])
            s_.activation(GB[:, 0:4], DCL[:, 4:8], AF.Ln)
            s_.activation(DCL[:, 8:12], GB[:, 0:4], AF.Exp,
                          scale=NEGC[:, 0:1]).then_inc(sc, 1)
            c.add("sc", 1)

        @blk.vector
        def _(v_):
            v_.wait_ge(sc, c.get("sc"))
            v_.tensor_tensor(SHD[:, 0:4], DCL[:, 4:8],
                             WSC[:, 5:6].to_broadcast([128, 4]), A.mult).then_inc(vv, 1)
            c.add("vv", 1)

        if tg == 0 and any(k in L["dbg"] for k in ("dcl", "gb4", "shd")):
            @blk.gpsimd
            def _(g_):
                g_.wait_ge(sc, c.get("sc"))
                for nm, sb in (("dcl", DCL), ("gb4", GB[:, 0:4]), ("shd", SHD)):
                    if nm in L["dbg"]:
                        g_.dma_start(out=L["dbg"][nm], in_=sb[:, 0:4] if nm == "gb4"
                                     else sb[:, :]).then_inc(st, 16)
                        c.add("st", 16)

        @blk.tensor
        def _(t_):
            t_.wait_ge(sc, c.get("sc"))
            ins = None
            for j in range(4):
                ins = t_.matmul(pD[5][0:1, j * 128:(j + 1) * 128], DCL[:, 8 + j:9 + j],
                                IDF[:], start=True, stop=True)
            ins.then_inc(mm, 1)
            c.add("mm", 1)

        @blk.vector
        def _(v_):
            v_.wait_ge(mm, c.get("mm"))
            v_.tensor_copy(r2, pD[5][0:1, 0:512]).then_inc(vv, 1)
            c.add("vv", 1)

        @blk.tensor
        def _(t_):
            t_.wait_ge(vv, c.get("vv"))
            t_.matmul(pD[4][:, 0:512], ONE1[:], r2, start=True, stop=True).then_inc(mm, 1)
            c.add("mm", 1)

        @blk.vector
        def _(v_):
            v_.wait_ge(mm, c.get("mm"))
            v_.tensor_copy(RSB[:], pD[4][:, 0:512]).then_inc(vv, 1)
            c.add("vv", 1)

    with nc.Block() as blk:
        @blk.vector
        def _(v_):
            if tg == 1:
                v_.wait_ge(mm, c.v["mm_down0"])
            for ic in range(IC):
                v_.tensor_tensor(GB[:], HH[:, ic, :], RSB[:], A.mult)
                v_.tensor_scalar(hq_ap(ic), GB[:], MAGIC, MAGIC, A.add, A.subtract)
            v_.engine_nop().then_inc(vv, 1)
            c.add("vv", 1)

    base_sl0 = c.get("sl0")
    base_sl1 = c.get("sl1")
    base_mm = c.get("mm")
    base_vv = c.get("vv")
    with nc.Block() as blk:
        @blk.sync
        def _(s_):
            n = 0
            for dfq in range(8):
                for kg in range(16):
                    if n % 2 == 0:
                        if n >= 2:
                            s_.wait_ge(mm, base_mm + n - 1)
                        s_.dma_start(out=WD[0][:],
                                     in_=wqd.ap()[kg * 512:(kg + 1) * 512,
                                                  dfq * 256:(dfq + 1) * 256].rearrange(
                                         "(a p) c -> p a c", p=128)).then_inc(slp[0], 16)
                        c.add("sl0", 16)
                    n += 1

        @blk.scalar
        def _(s_):
            n = 0
            for dfq in range(8):
                for kg in range(16):
                    if n % 2 == 1:
                        if n >= 2:
                            s_.wait_ge(mm, base_mm + n - 1)
                        s_.dma_start(out=WD[1][:],
                                     in_=wqd.ap()[kg * 512:(kg + 1) * 512,
                                                  dfq * 256:(dfq + 1) * 256].rearrange(
                                         "(a p) c -> p a c", p=128)).then_inc(slp[1], 16)
                        c.add("sl1", 16)
                    n += 1

        @blk.tensor
        def _(t_):
            t_.wait_ge(vv, base_vv)
            n = 0
            for dfq in range(8):
                if dfq >= 1:
                    t_.wait_ge(vv, base_vv + dfq)
                for kg in range(16):
                    bsl = base_sl0 if n % 2 == 0 else base_sl1
                    t_.wait_ge(slp[n % 2], bsl + 16 * (n // 2 + 1))
                    for kkl in range(4):
                        kk = kg * 4 + kkl
                        for tok in range(4):
                            ins = t_.matmul(pD[tok][:, 0:256],
                                            hq_ap(kk)[:, tok * 128:(tok + 1) * 128],
                                            WD[n % 2][:, kkl, :],
                                            start=(kk == 0), stop=(kk == IC - 1))
                    ins.then_inc(mm, 1)
                    c.add("mm", 1)
                    n += 1

        @blk.vector
        def _(v_):
            dsb = HH.ap().rearrange("p a t -> p (a t)").bitcast(F32)
            for dfq in range(8):
                v_.wait_ge(mm, base_mm + 16 * (dfq + 1))
                for tok in range(4):
                    col = tok * 2048 + dfq * 256
                    v_.tensor_scalar(dsb[:, col:col + 256], pD[tok][:, 0:256],
                                     SHD[:, tok:tok + 1], None, A.mult)
                v_.engine_nop().then_inc(vv, 1)
                c.add("vv", 1)

        @blk.gpsimd
        def _(g_):
            dsb = HH.ap().rearrange("p a t -> p (a t)").bitcast(F32)
            g_.wait_ge(vv, c.get("vv"))
            g_.wait_ge(stp[0], c.v["st_C0"])
            g_.wait_ge(stp[1], c.v["st_C1"])
            for tok in range(4):
                r0 = tg * 512 + tok * 128
                g_.dma_start(out=out[r0:r0 + 128, :],
                             in_=dsb[:, tok * 2048:(tok + 1) * 2048],
                             accum_op=A.add).then_inc(acc, 16)
            c.add("acc", 16 * 4)
    if tg == 0:
        c.v["mm_down0"] = c.get("mm")
        c.v["acc_tg0"] = c.get("acc")


def host_consts():
    i = np.arange(128, dtype=np.float32)
    dt = -(i[None, :] - i[:, None])
    dt[dt > 0] = NEG_BIG
    kd = (-(128.0 - i))[:, None].copy()
    qd = np.tile(-i[None, :], (128, 1))
    qdg = -np.arange(T, dtype=np.float32)[None, :]
    ident = np.eye(128, dtype=np.float32)
    return (dt.astype(np.float32), kd.astype(np.float32), qd.astype(np.float32),
            qdg.astype(np.float32), ident)


class Ctr:
    def __init__(self):
        self.v = {}

    def add(self, k, n=1):
        self.v[k] = self.v.get(k, 0) + n
        return self.v[k]

    def get(self, k):
        return self.v.get(k, 0)


def build_nc(debug=()):
    import ml_dtypes
    nc = bass.Bass("TRN2", target_bir_lowering=False, detect_race_conditions=False)
    c = Ctr()

    xf = nc.dram_tensor("xf", [D, T], F32, kind="ExternalInput").ap()
    xt = nc.dram_tensor("xt", [T, D], F32, kind="ExternalInput").ap()
    wqkv_s = nc.dram_tensor("wqkv_s", [256, 3 * D], F32, kind="ExternalInput").ap()
    wout_s = nc.dram_tensor("wout_s", [256, D], F32, kind="ExternalInput").ap()
    bqkv_pp = nc.dram_tensor("bqkv_pp", [128, 32], F32, kind="ExternalInput").ap()
    bv_row = nc.dram_tensor("bv_row", [1, D], F32, kind="ExternalInput").ap()
    bout_row = nc.dram_tensor("bout_row", [1, D], F32, kind="ExternalInput").ap()
    slopes = nc.dram_tensor("slopes", [1, H], F32, kind="ExternalInput").ap()
    pdist = nc.dram_tensor("pdist", [1, NCORE], F32, kind="ExternalInput").ap()
    wgs = nc.dram_tensor("wgs", [256, I], F32, kind="ExternalInput").ap()
    wus = nc.dram_tensor("wus", [256, I], F32, kind="ExternalInput").ap()
    wds = nc.dram_tensor("wds", [1024, D], F32, kind="ExternalInput").ap()
    out = nc.dram_tensor("out", [T, D], F32, kind="ExternalOutput").ap()

    dtneg_h, kdneg_h, qdneg_h, qdgneg_h, ident_h = host_consts()
    dtneg = nc.inline_tensor(dtneg_h, name="dtneg").ap()
    kdneg = nc.inline_tensor(kdneg_h, name="kdneg").ap()
    qdneg = nc.inline_tensor(qdneg_h, name="qdneg").ap()
    qdgneg = nc.inline_tensor(qdgneg_h, name="qdgneg").ap()
    identbf = nc.inline_tensor(ident_h.astype(ml_dtypes.bfloat16), name="identbf").ap()
    identf32 = nc.inline_tensor(ident_h, name="identf32").ap()

    wpar_in = nc.dram_tensor("wpar_in", [1, 16], F32)
    wpar_out = nc.dram_tensor("wpar_out", [NCORE, 16], F32)
    wqkvb_in = nc.dram_tensor("wqkvb_in", [256, 3 * D], BF16)
    wqkvb = nc.dram_tensor("wqkvb", [D, 3 * D], BF16)
    woutb_in = nc.dram_tensor("woutb_in", [256, D], BF16)
    woutb = nc.dram_tensor("woutb", [D, D], BF16)
    wqg_in = nc.dram_tensor("wqg_in", [256, I], BF16)
    wqg = nc.dram_tensor("wqg", [D, I], BF16)
    wqu_in = nc.dram_tensor("wqu_in", [256, I], BF16)
    wqu = nc.dram_tensor("wqu", [D, I], BF16)
    wqd_in = nc.dram_tensor("wqd_in", [1024, D], BF16)
    wqd = nc.dram_tensor("wqd", [I, D], BF16)
    e_in = nc.dram_tensor("e_in", [H * HD, HD], F32)
    e_out = nc.dram_tensor("e_out", [NCORE * H * HD, HD], F32)

    dbg = {}
    for name, shape, dty in (("att", [H * 128, T], BF16), ("q", [H * 128, T], BF16),
                             ("xq", [KC * 128, T], BF16), ("k", [H * 128, TG], BF16),
                             ("e", [H * 128, HD], F32), ("xnb", [KC * 128, T], BF16),
                             ("q0", [H * 128, TG], BF16), ("q1", [H * 128, TG], BF16),
                             ("wqg", [2048, 8192], BF16), ("wqu", [2048, 8192], BF16),
                             ("wqd", [8192, 2048], BF16), ("wqkvb", [2048, 6144], BF16),
                             ("woutb", [2048, 2048], BF16),
                             ("wsc", [128, 6], F32), ("wsr", [1, 8], F32),
                             ("sab", [128, 16], F32), ("sal", [8, 16], F32),
                             ("wsr3", [1, 8], F32), ("snp", [1, 16], F32),
                             ("dcl", [128, 12], F32), ("gb4", [128, 4], F32),
                             ("shd", [128, 4], F32)):
        if name in debug:
            dbg[name] = nc.dram_tensor("dbg_" + name, shape, dty, kind="ExternalOutput").ap()

    es = ExitStack()
    with es:
        XQ = es.enter_context(nc.sbuf_tensor("XQ", [128, KC, T], BF16))
        QF = es.enter_context(nc.sbuf_tensor("QF", [128, H, T], BF16))
        ATT = es.enter_context(nc.sbuf_tensor("ATT", [128, H, T], BF16))
        WSC = es.enter_context(nc.sbuf_tensor("WSC", [128, 6], F32))
        XSR = es.enter_context(nc.sbuf_tensor("XSR", [1, T], F32))
        WSR = es.enter_context(nc.sbuf_tensor("WSR", [1, 8], F32))
        SNP = es.enter_context(nc.sbuf_tensor("SNP", [1, 16], F32))
        NEGC = es.enter_context(nc.sbuf_tensor("NEGC", [128, 1], F32))
        RC127 = es.enter_context(nc.sbuf_tensor("RC127", [128, 1], F32))
        EPSC = es.enter_context(nc.sbuf_tensor("EPSC", [128, 1], F32))
        ONE1 = es.enter_context(nc.sbuf_tensor("ONE1", [1, 128], F32))
        ONEC = es.enter_context(nc.sbuf_tensor("ONEC", [128, 1], F32))
        ONE11 = es.enter_context(nc.sbuf_tensor("ONE11", [1, 1], F32))
        ROW = es.enter_context(nc.sbuf_tensor("ROW", [1, 1312], F32))
        IDF = es.enter_context(nc.sbuf_tensor("IDF", [128, 128], F32))

        SEM = {}
        for s in ["sx", "sl", "gp", "mm", "vv", "sc", "cc", "st", "acc",
                  "sx0", "sx1", "sl0", "sl1", "gp0", "gp1", "st0", "st1",
                  "ccp", "ccc", "ccw", "cce"]:
            SEM[s] = es.enter_context(nc.semaphore("sem_" + s))
        sx, sl, gp, mm, vv, sc, cc, st, acc = (SEM[k] for k in
            ["sx", "sl", "gp", "mm", "vv", "sc", "cc", "st", "acc"])
        sxp = [SEM["sx0"], SEM["sx1"]]
        slp = [SEM["sl0"], SEM["sl1"]]
        gpp = [SEM["gp0"], SEM["gp1"]]
        stp = [SEM["st0"], SEM["st1"]]
        ccp, ccc, ccw, cce = SEM["ccp"], SEM["ccc"], SEM["cce"], SEM["cce"]
        ccw = SEM["ccw"]

        ph = ExitStack()
        with ph:
            st0 = ph.enter_context(nc.sbuf_tensor("st0", [128, 4096], F32))
            st1 = ph.enter_context(nc.sbuf_tensor("st1", [128, 4096], F32))
            sq0 = ph.enter_context(nc.sbuf_tensor("sq0", [128, 4096], BF16))
            sq1 = ph.enter_context(nc.sbuf_tensor("sq1", [128, 4096], BF16))
            sabw = ph.enter_context(nc.sbuf_tensor("sabw", [128, 16], F32))
            sal = ph.enter_context(nc.sbuf_tensor("sal", [NCORE, 16], F32))
            pS = [ph.enter_context(nc.psum_tensor(f"pS{i}", [128, 512], F32))
                  for i in range(2)]
            stl = [st0, st1]
            sqt = [sq0, sq1]

            def pieces(src_ap, dst, nrow_tiles, total_cols):
                ps = []
                for t in range(nrow_tiles):
                    c0 = 0
                    while c0 < total_cols:
                        cw = min(4096, total_cols - c0)
                        ps.append((src_ap, dst, t * 128, c0, cw))
                        c0 += cw
                return ps

            cast_pieces = (pieces(wqkv_s, wqkvb_in, 2, 3 * D) +
                           pieces(wout_s, woutb_in, 2, D))
            with nc.Block() as blk:
                @blk.sync
                def _(s_):
                    for i, (sp, dst, r0, c0, cw) in enumerate(cast_pieces):
                        if i >= 2:
                            s_.wait_ge(vv, i - 1)
                        s_.dma_start(out=stl[i % 2][:, 0:cw],
                                     in_=sp[r0:r0 + 128, c0:c0 + cw]).then_inc(sxp[i % 2], 16)
                        c.add("sx%d" % (i % 2), 16)

                @blk.vector
                def _(v_):
                    v_.memset(ONE1[:], 1.0)
                    v_.memset(ONEC[:], 1.0)
                    v_.memset(ONE11[:], 1.0)
                    v_.memset(NEGC[:], -1.0)
                    v_.memset(RC127[:], 1.0 / 127.0)
                    v_.memset(EPSC[:], EPS)
                    v_.memset(SNP[0:1, 8:9], 1.0 / 16777216.0)
                    v_.memset(SNP[0:1, 9:10], EPS)
                    for i, (sp, dst, r0, c0, cw) in enumerate(cast_pieces):
                        v_.wait_ge(sxp[i % 2], 16 * (i // 2 + 1))
                        if i >= 2:
                            v_.wait_ge(gpp[i % 2], 16 * (i // 2))
                        v_.tensor_copy(sqt[i % 2][:, 0:cw],
                                       stl[i % 2][:, 0:cw]).then_inc(vv, 1)
                        c.add("vv", 1)

                @blk.gpsimd
                def _(g_):
                    for i, (sp, dst, r0, c0, cw) in enumerate(cast_pieces):
                        g_.wait_ge(vv, i + 1)
                        g_.dma_start(out=dst[r0:r0 + 128, c0:c0 + cw],
                                     in_=sqt[i % 2][:, 0:cw]).then_inc(gpp[i % 2], 16)
                        c.add("gp%d" % (i % 2), 16)
                    g_.wait_ge(gpp[0], c.get("gp0"))
                    g_.wait_ge(gpp[1], c.get("gp1"))
                    for dst, gath in [(wqkvb_in, wqkvb), (woutb_in, woutb)]:
                        g_.collective_compute("AllGather", A.bypass, replica_groups=RG,
                                              ins=[dst.ap().opt()], outs=[gath.ap().opt()]
                                              ).then_inc(ccc, 1)
                        c.add("ccc", 1)
            c.v["cc_cast"] = c.get("ccc")

            ffn_pieces = []
            for sp, dst, nt, cols, wi in [(wgs, wqg_in, 2, I, 0), (wus, wqu_in, 2, I, 1),
                                          (wds, wqd_in, 8, D, 2)]:
                for t in range(nt):
                    c0 = 0
                    while c0 < cols:
                        cw = min(4096, cols - c0)
                        ffn_pieces.append((sp, dst, t * 128, c0, cw, wi))
                        c0 += cw
            NP = len(ffn_pieces)
            with nc.Block() as blk:
                @blk.sync
                def _(s_):
                    base_vv = c.get("vv")
                    for i, (sp, dst, r0, c0, cw, wi) in enumerate(ffn_pieces):
                        if i % 2 != 0:
                            continue
                        s_.wait_ge(vv, base_vv + (i - 1 if i >= 2 else 0))
                        s_.dma_start(out=stl[0][:, 0:cw],
                                     in_=sp[r0:r0 + 128, c0:c0 + cw]).then_inc(sxp[0], 16)
                        c.add("sx0", 16)

                @blk.scalar
                def _(s_):
                    base_vv = c.get("vv")
                    for i, (sp, dst, r0, c0, cw, wi) in enumerate(ffn_pieces):
                        if i % 2 != 1:
                            continue
                        s_.wait_ge(vv, base_vv + (i - 1 if i >= 2 else 0))
                        s_.dma_start(out=stl[1][:, 0:cw],
                                     in_=sp[r0:r0 + 128, c0:c0 + cw]).then_inc(sxp[1], 16)
                        c.add("sx1", 16)

                @blk.vector
                def _(v_):
                    base0 = c.get("sx0") - 16 * ((NP + 1) // 2)
                    base1 = c.get("sx1") - 16 * (NP // 2)
                    for i, (sp, dst, r0, c0, cw, wi) in enumerate(ffn_pieces):
                        b = base0 if i % 2 == 0 else base1
                        v_.wait_ge(sxp[i % 2], b + 16 * (i // 2 + 1))
                        v_.tensor_reduce(sabw[:, i:i + 1], stl[i % 2][:, 0:cw], AX.X,
                                         A.add, apply_absolute_value=True
                                         ).then_inc(vv, 1)
                        c.add("vv", 1)

            with nc.Block() as blk:
                @blk.tensor
                def _(t_):
                    t_.wait_ge(vv, c.get("vv"))
                    t_.matmul(pS[0][0:1, 0:16], ONEC[:], sabw[:, 0:16], start=True,
                              stop=True).then_inc(mm, 1)
                    c.add("mm", 1)

                @blk.scalar
                def _(s_):
                    s_.wait_ge(mm, c.get("mm"))
                    s_.copy(ROW[0:1, 0:16], pS[0][0:1, 0:16]).then_inc(sc, 1)
                    c.add("sc", 1)

                @blk.gpsimd
                def _(g_):
                    g_.wait_ge(sc, c.get("sc"))
                    g_.dma_start(out=wpar_in.ap(), in_=ROW[0:1, 0:16]).then_inc(gp, 16)
                    c.add("gp", 16)
                    g_.wait_ge(gp, c.get("gp"))
                    g_.collective_compute("AllGather", A.bypass, replica_groups=RG,
                                          ins=[wpar_in.ap().opt()],
                                          outs=[wpar_out.ap().opt()]).then_inc(ccp, 1)
                    c.add("ccp", 1)
                    g_.wait_ge(ccp, c.get("ccp"))
                    g_.dma_start(out=sal[:, :], in_=wpar_out.ap()).then_inc(gp, 16)
                    c.add("gp", 16)

            with nc.Block() as blk:
                @blk.tensor
                def _(t_):
                    t_.wait_ge(gp, c.get("gp"))
                    t_.matmul(pS[0][0:1, 0:16], ONEC[0:NCORE, :], sal[:, :], start=True,
                              stop=True).then_inc(mm, 1)
                    c.add("mm", 1)

                @blk.vector
                def _(v_):
                    v_.wait_ge(mm, c.get("mm"))
                    v_.tensor_reduce(WSR[0:1, 0:1], pS[0][0:1, 0:4], AX.X, A.add)
                    v_.tensor_reduce(WSR[0:1, 1:2], pS[0][0:1, 4:8], AX.X, A.add)
                    v_.tensor_reduce(WSR[0:1, 2:3], pS[0][0:1, 8:16], AX.X,
                                     A.add).then_inc(vv, 1)
                    c.add("vv", 1)

                @blk.scalar
                def _(s_):
                    s_.wait_ge(vv, c.get("vv"))
                    s_.activation(WSR[0:1, 0:3], WSR[0:1, 0:3], AF.Abs,
                                  bias=SNP[0:1, 9:10], scale=SNP[0:1, 8:9])
                    s_.activation(SNP[0:1, 0:3], WSR[0:1, 0:3], AF.Ln)
                    s_.activation(WSR[0:1, 4:7], SNP[0:1, 0:3], AF.Exp,
                                  scale=NEGC[0:1, 0:1]).then_inc(sc, 1)
                    c.add("sc", 1)

                @blk.tensor
                def _(t_):
                    t_.wait_ge(sc, c.get("sc"))
                    t_.matmul(pS[1][:, 0:3], ONE1[:], WSR[0:1, 4:7], start=True, stop=True)
                    t_.matmul(pS[1][:, 3:6], ONE1[:], WSR[0:1, 0:3], start=True,
                              stop=True).then_inc(mm, 1)
                    c.add("mm", 1)

                if "sab" in dbg or "sal" in dbg or "wsr3" in dbg:
                    @blk.gpsimd
                    def _(g_):
                        g_.wait_ge(sc, c.get("sc"))
                        for nm, sb in (("sab", sabw), ("sal", sal), ("wsr3", WSR)):
                            if nm in dbg:
                                g_.dma_start(out=dbg[nm], in_=sb[:, :]).then_inc(st, 16)
                                c.add("st", 16)

                @blk.scalar
                def _(s_):
                    s_.wait_ge(mm, c.get("mm"))
                    s_.copy(WSC[:, 0:6], pS[1][:, 0:6]).then_inc(sc, 1)
                    c.add("sc", 1)

            with nc.Block() as blk:
                @blk.sync
                def _(s_):
                    base_vv = c.get("vv")
                    for i, (sp, dst, r0, c0, cw, wi) in enumerate(ffn_pieces):
                        if i % 2 != 0:
                            continue
                        s_.wait_ge(vv, base_vv + (i - 1 if i >= 2 else 0))
                        s_.dma_start(out=stl[0][:, 0:cw],
                                     in_=sp[r0:r0 + 128, c0:c0 + cw]).then_inc(sxp[0], 16)
                        c.add("sx0", 16)

                @blk.scalar
                def _(s_):
                    base_vv = c.get("vv")
                    for i, (sp, dst, r0, c0, cw, wi) in enumerate(ffn_pieces):
                        if i % 2 != 1:
                            continue
                        s_.wait_ge(vv, base_vv + (i - 1 if i >= 2 else 0))
                        s_.dma_start(out=stl[1][:, 0:cw],
                                     in_=sp[r0:r0 + 128, c0:c0 + cw]).then_inc(sxp[1], 16)
                        c.add("sx1", 16)

                @blk.vector
                def _(v_):
                    base0 = c.get("sx0") - 16 * ((NP + 1) // 2)
                    base1 = c.get("sx1") - 16 * (NP // 2)
                    bg0 = c.get("gp0")
                    bg1 = c.get("gp1")
                    v_.wait_ge(sc, c.get("sc"))
                    for i, (sp, dst, r0, c0, cw, wi) in enumerate(ffn_pieces):
                        bx = base0 if i % 2 == 0 else base1
                        v_.wait_ge(sxp[i % 2], bx + 16 * (i // 2 + 1))
                        bgp = bg0 if i % 2 == 0 else bg1
                        v_.wait_ge(gpp[i % 2], bgp + 16 * (i // 2))
                        b = stl[i % 2][:, 0:cw]
                        v_.tensor_scalar(b, b, WSC[:, wi:wi + 1], MAGIC, A.mult, A.add)
                        v_.tensor_scalar(b, b, MAGIC, 1.0, A.subtract, A.min)
                        v_.tensor_scalar(sqt[i % 2][:, 0:cw], b, -1.0, None,
                                         A.max).then_inc(vv, 1)
                        c.add("vv", 1)

                @blk.gpsimd
                def _(g_):
                    base_vv2 = c.get("vv") - NP
                    for i, (sp, dst, r0, c0, cw, wi) in enumerate(ffn_pieces):
                        g_.wait_ge(vv, base_vv2 + i + 1)
                        g_.dma_start(out=dst[r0:r0 + 128, c0:c0 + cw],
                                     in_=sqt[i % 2][:, 0:cw]).then_inc(gpp[i % 2], 16)
                        c.add("gp%d" % (i % 2), 16)
                    g_.wait_ge(gpp[0], c.get("gp0"))
                    g_.wait_ge(gpp[1], c.get("gp1"))
                    for dst, gath in [(wqg_in, wqg), (wqu_in, wqu), (wqd_in, wqd)]:
                        g_.collective_compute("AllGather", A.bypass, replica_groups=RG,
                                              ins=[dst.ap().opt()], outs=[gath.ap().opt()]
                                              ).then_inc(ccw, 1)
                        c.add("ccw", 1)
            c.v["cc_weights"] = c.get("ccw")

        with nc.Block() as blk:
            for eng in ("sync", "scalar", "vector", "tensor", "gpsimd"):
                @getattr(blk, eng)
                def _(e_):
                    e_.wait_ge(gpp[0], c.get("gp0"))
                    e_.wait_ge(gpp[1], c.get("gp1"))
                    e_.wait_ge(gp, c.get("gp"))

        ph = ExitStack()
        with ph:
            EB = ph.enter_context(nc.sbuf_tensor("EB", [128, H, HD], F32))
            MSK = ph.enter_context(nc.sbuf_tensor("MSK", [128, H, HD], BF16))
            SLB = ph.enter_context(nc.sbuf_tensor("SLB", [128, H], F32))
            KDA = ph.enter_context(nc.sbuf_tensor("KDA", [128, H], F32))
            LMA = ph.enter_context(nc.sbuf_tensor("LMA", [128, H], F32))
            BPP = ph.enter_context(nc.sbuf_tensor("BPP", [128, 32], F32))
            IDB = ph.enter_context(nc.sbuf_tensor("IDB", [128, 128], BF16))
            QDE = ph.enter_context(nc.sbuf_tensor("QDE", [128, 128], F32))
            QD2 = ph.enter_context(nc.sbuf_tensor("QD2", [128, 128], F32))
            QDQ = ph.enter_context(nc.sbuf_tensor("QDQ", [128, TG], BF16))
            STB = ph.enter_context(nc.sbuf_tensor("STB", [128, 512], BF16))
            KTM = ph.enter_context(nc.sbuf_tensor("KTM", [128, 512], BF16))
            SBF = ph.enter_context(nc.sbuf_tensor("SBF", [128, 512], BF16))
            pA = [ph.enter_context(nc.psum_tensor(f"pA{i}", [128, 512], F32))
                  for i in range(8)]

            ph2 = ExitStack()
            with ph2:
                XFB = ph2.enter_context(nc.sbuf_tensor("XFB", [128, KC, HG], F32))
                XNB = ph2.enter_context(nc.sbuf_tensor("XNB", [128, KC, TG], BF16))
                KF = ph2.enter_context(nc.sbuf_tensor("KF", [128, H, TG], BF16))
                VT = ph2.enter_context(nc.sbuf_tensor("VT", [128, 4, D], BF16))
                WS0 = ph2.enter_context(nc.sbuf_tensor("WS0", [128, KC, 128], BF16))
                WS1 = ph2.enter_context(nc.sbuf_tensor("WS1", [128, KC, 128], BF16))
                BVB = ph2.enter_context(nc.sbuf_tensor("BVB", [128, D], BF16))
                TB0 = ph2.enter_context(nc.sbuf_tensor("TB0", [128, HG], F32))
                TB1 = ph2.enter_context(nc.sbuf_tensor("TB1", [128, HG], F32))
                RMX = ph2.enter_context(nc.sbuf_tensor("RMX", [128, HG], F32))
                RMN = ph2.enter_context(nc.sbuf_tensor("RMN", [128, HG], F32))
                MB = ph2.enter_context(nc.sbuf_tensor("MB", [128, HG], F32))
                RB = ph2.enter_context(nc.sbuf_tensor("RB", [128, HG], F32))
                CB = ph2.enter_context(nc.sbuf_tensor("CB", [128, HG], F32))
                BVR = ROW[0:1, 32:544]
                LNC = ph2.enter_context(nc.sbuf_tensor("LNC", [128, 2], F32))

                with nc.Block() as blk:
                    @blk.sync
                    def _(s_):
                        s_.dma_start(out=BPP[:], in_=bqkv_pp).then_inc(sx, 16)
                        s_.dma_start(out=ROW[0:1, 0:H], in_=slopes).then_inc(sx, 16)
                        s_.dma_start(out=IDB[:], in_=identbf).then_inc(sx, 16)
                        s_.dma_start(out=TB0[:, 0:1], in_=kdneg).then_inc(sx, 16)
                        s_.dma_start(out=QDE[:], in_=qdneg).then_inc(sx, 16)
                        s_.dma_start(out=TB1[:, 0:128], in_=dtneg).then_inc(sx, 16)
                        s_.dma_start(out=IDF[:], in_=identf32).then_inc(sx, 16)
                        c.add("sx", 16 * 7)
                        base_mm0 = c.get("mm")
                        for j in range(4):
                            if j >= 1:
                                s_.wait_ge(mm, base_mm0 + j)
                            s_.dma_start(out=BVR,
                                         in_=bv_row[0:1, j * 512:(j + 1) * 512]
                                         ).then_inc(sx, 16)
                        c.add("sx", 16 * 4)

                    @blk.tensor
                    def _(t_):
                        t_.wait_ge(sx, c.get("sx") - 16 * 4)
                        t_.matmul(pA[0][:, 0:H], ONE1[:], ROW[0:1, 0:H],
                                  start=True, stop=True)
                        base_sx0 = c.get("sx") - 16 * 4
                        base_mm0 = c.get("mm")
                        for j in range(4):
                            t_.wait_ge(sx, base_sx0 + 16 * (j + 1))
                            ins = t_.matmul(pA[1 + j][:, 0:512], ONE1[:], BVR,
                                            start=True, stop=True)
                            ins.then_inc(mm, 1)
                            c.add("mm", 1)

                    @blk.scalar
                    def _(s_):
                        s_.wait_ge(mm, c.get("mm"))
                        s_.copy(SLB[:], pA[0][:, 0:H])
                        for j in range(4):
                            s_.copy(BVB[:, j * 512:(j + 1) * 512], pA[1 + j][:, 0:512])
                        s_.copy(BPP[0:1, 0:1], BPP[0:1, 0:1]).then_inc(sc, 1)
                        c.add("sc", 1)

                    @blk.vector
                    def _(v_):
                        v_.wait_ge(sc, c.get("sc"))
                        v_.tensor_tensor(KDA[:], SLB[:], TB0[:, 0:1].to_broadcast([128, H]),
                                         A.mult)
                        v_.tensor_scalar(LMA[:], SLB[:], -128.0, None, A.mult
                                         ).then_inc(vv, 1)
                        c.add("vv", 1)
                        v_.memset(EB[:], 0.0).then_inc(vv, 1)
                        c.add("vv", 1)

                    @blk.scalar
                    def _(s_):
                        s_.wait_ge(vv, c.get("vv") - 1)
                        s_.activation(KDA[:], KDA[:], AF.Exp)
                        s_.activation(LMA[:], LMA[:], AF.Exp).then_inc(sc, 1)
                        c.add("sc", 1)

                with nc.Block() as blk:
                    @blk.vector
                    def _(v_):
                        v_.wait_ge(sc, c.get("sc"))
                        base_sc = c.get("sc")
                        for h in range(H):
                            if h >= 1:
                                v_.wait_ge(sc, base_sc + h)
                            v_.tensor_scalar(TB0[:, 0:128], TB1[:, 0:128],
                                             SLB[:, h:h + 1], None, A.mult).then_inc(vv, 1)
                            c.add("vv", 1)

                    @blk.scalar
                    def _(s_):
                        base_vv = c.get("vv") - H
                        for h in range(H):
                            s_.wait_ge(vv, base_vv + h + 1)
                            s_.activation(MSK[:, h, :], TB0[:, 0:128], AF.Exp
                                          ).then_inc(sc, 1)
                            c.add("sc", 1)

                L = dict(locals())
                for g in range(2):
                    for sub in range(2):
                        _emit_ln(nc, c, SEM, L, 2 * g + sub, (2 * g + sub) * HG)
                    _emit_qkv(nc, c, SEM, L, g)
                    if ("q%d" % g) in dbg:
                        with nc.Block() as blk:
                            @blk.sync
                            def _(s_, g=g):
                                s_.wait_ge(vv, c.get("vv"))
                                s_.dma_start(
                                    out=dbg["q%d" % g].rearrange("(k p) t -> p k t", p=128),
                                    in_=QF[:, :, g * 512:(g + 1) * 512]).then_inc(st, 16)
                                c.add("st", 16)
                    _emit_scan(nc, c, SEM, L, g)

                if "e" in dbg:
                    with nc.Block() as blk:
                        @blk.gpsimd
                        def _(g_):
                            g_.wait_ge(vv, c.get("vv"))
                            g_.dma_start(out=dbg["e"].rearrange("(h p) c -> p h c", p=128),
                                         in_=EB[:]).then_inc(gp, 16)
                            c.add("gp", 16)
                if "xnb" in dbg:
                    with nc.Block() as blk:
                        @blk.sync
                        def _(s_):
                            s_.wait_ge(vv, c.get("vv"))
                            s_.dma_start(out=dbg["xnb"][:, TG:2 * TG].rearrange(
                                "(k p) t -> p k t", p=128), in_=XNB[:]).then_inc(st, 16)
                            c.add("st", 16)

            ph3 = ExitStack()
            with ph3:
                EG = ph3.enter_context(nc.sbuf_tensor("EG", [128, H, HD], F32))
                PDB = ph3.enter_context(nc.sbuf_tensor("PDB", [128, NCORE], F32))
                WPB = ph3.enter_context(nc.sbuf_tensor("WPB", [128, H], F32))
                TBB = ph3.enter_context(nc.sbuf_tensor("TBB", [128, T], F32))
                EHB = ph3.enter_context(nc.sbuf_tensor("EHB", [128, T], F32))
                QGQ = ph3.enter_context(nc.sbuf_tensor("QGQ", [128, T], BF16))
                QGR = ph3.enter_context(nc.sbuf_tensor("QGR", [1, T], F32))
                L = dict(locals())
                _emit_prefix(nc, c, SEM, L)

            for name in ("att", "q", "xq"):
                if name in dbg:
                    src = {"att": ATT, "q": QF, "xq": XQ}[name]
                    with nc.Block() as blk:
                        @blk.sync
                        def _(s_, src=src, name=name):
                            s_.wait_ge(vv, c.get("vv"))
                            s_.dma_start(out=dbg[name].rearrange("(k p) t -> p k t", p=128),
                                         in_=src[:]).then_inc(st, 16)
                            c.add("st", 16)

        ph = ExitStack()
        with ph:
            WO0 = ph.enter_context(nc.sbuf_tensor("WO0", [128, KC, 512], BF16))
            WO1 = ph.enter_context(nc.sbuf_tensor("WO1", [128, KC, 512], BF16))
            BOB = ph.enter_context(nc.sbuf_tensor("BOB", [128, D], F32))
            BOR = ph.enter_context(nc.sbuf_tensor("BOR", [1, D], F32))
            XT0 = ph.enter_context(nc.sbuf_tensor("XT0", [128, 512], F32))
            XT1 = ph.enter_context(nc.sbuf_tensor("XT1", [128, 512], F32))
            OB0 = ph.enter_context(nc.sbuf_tensor("OB0", [128, 512], F32))
            OB1 = ph.enter_context(nc.sbuf_tensor("OB1", [128, 512], F32))
            pC = [ph.enter_context(nc.psum_tensor(f"pC{i}", [128, 512], F32))
                  for i in range(8)]
            L = dict(locals())
            _emit_wout(nc, c, SEM, L)

        ph = ExitStack()
        with ph:
            HH = ph.enter_context(nc.sbuf_tensor("HH", [128, IC, TG], FP16))
            WG0 = ph.enter_context(nc.sbuf_tensor("WG0", [128, KC, 128], BF16))
            WG1 = ph.enter_context(nc.sbuf_tensor("WG1", [128, KC, 128], BF16))
            WU0 = ph.enter_context(nc.sbuf_tensor("WU0", [128, KC, 128], BF16))
            WU1 = ph.enter_context(nc.sbuf_tensor("WU1", [128, KC, 128], BF16))
            WD0 = ph.enter_context(nc.sbuf_tensor("WD0", [128, 4, 256], BF16))
            WD1 = ph.enter_context(nc.sbuf_tensor("WD1", [128, 4, 256], BF16))
            BG = ph.enter_context(nc.sbuf_tensor("BG", [128, TG], F32))
            BU = ph.enter_context(nc.sbuf_tensor("BU", [128, TG], F32))
            RSB = ph.enter_context(nc.sbuf_tensor("RSB", [128, TG], F32))
            GB = ph.enter_context(nc.sbuf_tensor("GB", [128, TG], F32))
            UB = ph.enter_context(nc.sbuf_tensor("UB", [128, TG], F32))
            RXH = ph.enter_context(nc.sbuf_tensor("RXH", [128, TG], F32))
            RNH = ph.enter_context(nc.sbuf_tensor("RNH", [128, TG], F32))
            SHD = ph.enter_context(nc.sbuf_tensor("SHD", [128, 4], F32))
            DCL = ph.enter_context(nc.sbuf_tensor("DCL", [128, 12], F32))
            pD = [ph.enter_context(nc.psum_tensor(f"pD{i}", [128, 512], F32))
                  for i in range(8)]
            L = dict(locals())
            for tg in range(2):
                _emit_ffn(nc, c, SEM, L, tg)

        wtaps = [("wqg", "wqg"), ("wqu", "wqu"), ("wqd", "wqd"),
                 ("wqkvb", "wqkvb"), ("woutb", "woutb")]
        if any(n in dbg for n, _ in wtaps) or "wsc" in dbg or "wsr" in dbg:
            drams = {"wqg": wqg, "wqu": wqu, "wqd": wqd, "wqkvb": wqkvb,
                     "woutb": woutb}
            with nc.Block() as blk:
                @blk.gpsimd
                def _(g_):
                    g_.wait_ge(acc, c.get("acc"))
                    for n, key in wtaps:
                        if n in dbg:
                            g_.dma_start(out=dbg[n], in_=drams[key].ap()
                                         ).then_inc(st, 16)
                            c.add("st", 16)
                    if "wsc" in dbg:
                        g_.dma_start(out=dbg["wsc"], in_=WSC[:, :]).then_inc(st, 16)
                        c.add("st", 16)
                    if "wsr" in dbg:
                        g_.dma_start(out=dbg["wsr"], in_=WSR[:, :]).then_inc(st, 16)
                        c.add("st", 16)
                    if "snp" in dbg:
                        g_.dma_start(out=dbg["snp"], in_=SNP[:, :]).then_inc(st, 16)
                        c.add("st", 16)

        with nc.Block() as blk:
            @blk.gpsimd
            def _(g_):
                g_.wait_ge(acc, c.get("acc"))
                g_.wait_ge(st, c.get("st"))
                g_.wait_ge(stp[0], c.get("st0"))
                g_.wait_ge(stp[1], c.get("st1"))

    return nc, dbg, c


def stage_inputs(ins, core):
    x = np.asarray(ins["L0_attn_x"], dtype=np.float32).reshape(8192, 2048)
    xt_full = np.ascontiguousarray(x)
    xf_full = np.ascontiguousarray(x.T)
    wqkv = np.ascontiguousarray(np.asarray(ins["L0_attn_Wqkv"], dtype=np.float32))
    wout = np.ascontiguousarray(np.asarray(ins["L0_attn_Wout"], dtype=np.float32))
    bqkv = np.asarray(ins["L0_attn_bqkv"], dtype=np.float32)
    bout = np.asarray(ins["L0_attn_bout"], dtype=np.float32)
    slopes = np.asarray(ins["L0_attn_slopes"], dtype=np.float32)
    wg = np.ascontiguousarray(np.asarray(ins["L0_ffn_w_gate"], dtype=np.float32).T)
    wu = np.ascontiguousarray(np.asarray(ins["L0_ffn_w_up"], dtype=np.float32).T)
    wd = np.ascontiguousarray(np.asarray(ins["L0_ffn_w_down"], dtype=np.float32).T)

    grp, mem = core // 4, core % 4
    pd = np.full((1, NCORE), NEG_BIG, dtype=np.float32)
    for p in range(NCORE):
        if p // 4 == grp and p % 4 < mem:
            pd[0, p] = -1024.0 * (mem - 1 - (p % 4))

    t0 = core * 1024
    return {
        "xf": np.ascontiguousarray(xf_full[:, t0:t0 + 1024]),
        "xt": np.ascontiguousarray(xt_full[t0:t0 + 1024, :]),
        "wqkv_s": np.ascontiguousarray(wqkv[core * 256:(core + 1) * 256, :]),
        "wout_s": np.ascontiguousarray(wout[core * 256:(core + 1) * 256, :]),
        "bqkv_pp": np.ascontiguousarray(bqkv[0:4096].reshape(32, 128).T),
        "bv_row": np.ascontiguousarray(bqkv[4096:6144].reshape(1, 2048)),
        "bout_row": np.ascontiguousarray(bout.reshape(1, 2048)),
        "slopes": np.ascontiguousarray(slopes.reshape(1, 16)),
        "pdist": pd,
        "wgs": np.ascontiguousarray(wg[core * 256:(core + 1) * 256, :]),
        "wus": np.ascontiguousarray(wu[core * 256:(core + 1) * 256, :]),
        "wds": np.ascontiguousarray(wd[core * 1024:(core + 1) * 1024, :]),
    }



def kernel(**inputs):
    from concourse.bass_utils import run_bass_kernel_spmd
    nc, dbg, c = build_nc()
    in_maps = [stage_inputs(inputs, core) for core in range(NCORE)]
    res = run_bass_kernel_spmd(nc, in_maps, core_ids=list(range(NCORE)))
    outs = [np.asarray(r["out"]) for r in res.results]
    return np.concatenate(outs, axis=0).reshape(2, 4096, 2048)
